# version 01 (final)
# baseline (speedup 1.0000x reference)
import numpy as np
import ml_dtypes
from contextlib import ExitStack

import concourse.bass as bass
import concourse.mybir as mybir
from concourse.bass_utils import run_bass_kernel_spmd

F32 = mybir.dt.float32
BF16 = mybir.dt.bfloat16
I32 = mybir.dt.int32
AX = mybir.AxisListType.X
AF = mybir.ActivationFunctionType
ALU = mybir.AluOpType

D, T, TH, KC, DH = 1024, 4096, 2048, 8, 256
NE = 8
BIGOFF = 1024.0


RACE_DEBUG = False
DEBUG_OUT = False
SELF_WAIT = True


def build_nc():
    nc = bass.Bass("TRN2", target_bir_lowering=False,
                   detect_race_conditions=RACE_DEBUG)
    es = ExitStack()

    def din(name, shape, dt=F32):
        return nc.dram_tensor(name, shape, dt, kind="ExternalInput").ap()

    DBG_SPILLS = {"xnn_hbm", "g0_hbm", "s0_hbm", "x1n_hbm",
                  "xn1n_hbm", "g1_hbm", "x1o_hbm", "qkv1_hbm",
                  "dbg_gre", "dbg_xnb", "dbg_h", "dbg_o", "dbg_gt", "dbg_y", "dbg_y1", "dbg_rt", "dbg_ye", "dbg_h2"}

    def dint(name, shape, dt):
        kind = "ExternalOutput" if (DEBUG_OUT and name in DBG_SPILLS) else "Internal"
        return nc.dram_tensor(name, shape, dt, kind=kind).ap()

    def sb(name, shape, dt):
        return es.enter_context(nc.sbuf_tensor(name, shape, dt))

    x0_own = din("x0_own", [KC, 128, TH])
    x0_h0 = din("x0_h0", [KC, 128, TH])
    x0_h1 = din("x0_h1", [KC, 128, TH])
    x0hg = din("x0hg", [4, 128, T])
    WwT_sl = din("WwT_sl", [KC, 128, 2 * DH])
    WrT_t = din("WrT_t", [KC, 128, D])
    dec_al = din("dec_al", [1, 4])
    WgT = din("WgT", [KC, 128, NE])
    W1c = din("W1c", [NE, 32, 128, KC * 128])
    b1c = din("b1c", [NE, 128, 32])
    W2c = din("W2c", [NE, KC, 128, 32 * 128])
    b2c = din("b2c", [NE, 128, KC])
    WqT_t = din("WqT_t", [KC, 128, D])
    Wqkv_sl = din("Wqkv_sl", [KC, 128, 12 * 128])
    bqkv_sl = din("bqkv_sl", [128, 12])
    Wop_t = din("Wop_t", [2, 128, D])
    W2a = din("W2a", [128, 2])
    b2a = din("b2a", [1, 1])
    wins = din("wins", [1, 2])
    gWT1 = din("gWT1", [KC, 128, NE])
    W1c1 = din("W1c1", [NE, 4, 128, KC * 128])
    b1c1 = din("b1c1", [NE, 128, 4])
    W2c1 = din("W2c1", [NE, KC, 128, 4 * 128])
    b2c1 = din("b2c1", [NE, 128, KC])
    Ws1c = din("Ws1c", [4, 128, KC * 128])
    bs1c = din("bs1c", [128, 4])
    Ws2c = din("Ws2c", [KC, 128, 4 * 128])
    bs2c = din("bs2c", [128, KC])
    rsc = din("rsc", [1, 1])
    ones_row = din("ones_row", [1, 128])
    ones_col = din("ones_col", [128, 1])
    identf = din("identf", [128, 128])
    Dbase_c = din("Dbase_c", [128, 512])
    D64_c = din("D64_c", [64, 64])
    eye4_c = din("eye4_c", [64, 256])
    strL_c = din("strL_c", [64, 64])
    strU_c = din("strU_c", [64, 64])
    prow_c = din("prow_c", [1, 1024])
    tail_c = din("tail_c", [1, 64])
    hfidx = din("hfidx", [1, 2], I32)
    sel8_c = din("sel8_c", [8, 8 * 128], BF16)

    y_out = nc.dram_tensor("y_out", [KC, 128, TH], F32, kind="ExternalOutput").ap()

    w1cache = dint("w1cache", [NE, 32, 128, KC * 128], BF16)
    w2cache = dint("w2cache", [NE, KC, 128, 32 * 128], BF16)
    ag1a_in = dint("ag1a_in", [4 * 128, T], BF16)
    ag1a_out = dint("ag1a_out", [2 * 4 * 128, T], BF16)
    ag1b_in = dint("ag1b_in", [KC * 128, TH], BF16)
    ag1b_out = dint("ag1b_out", [2 * KC * 128, TH], BF16)
    ag2_in = dint("ag2_in", [4 * 128, T], BF16)
    ag2_out = dint("ag2_out", [2 * 4 * 128, T], BF16)
    xnn_hbm = dint("xnn_hbm", [KC, 128, T], BF16)
    xh_hbm = dint("xh_hbm", [4, 128, T], BF16)
    s0_hbm = dint("s0_hbm", [KC, 128, T], F32)
    g0_hbm = dint("g0_hbm", [KC, 128, TH], F32)
    x1n_hbm = dint("x1n_hbm", [KC, 128, T], BF16)
    xn1n_hbm = dint("xn1n_hbm", [KC, 128, T], BF16)
    x1o_hbm = dint("x1o_hbm", [KC, 128, TH], F32)
    g1_hbm = dint("g1_hbm", [KC, 128, TH], F32)
    qkv1_hbm = dint("qkv1_hbm", [6, 128, T], BF16)
    stat1_hbm = dint("stat1_hbm", [1, 2 * T], F32)
    dbg_gre = dint("dbg_gre", [8, TH], BF16)
    dbg_xnb = dint("dbg_xnb", [128, 4096], BF16)
    dbg_h = dint("dbg_h", [128, 2048], BF16)
    dbg_o = dint("dbg_o", [128, 512], BF16)
    dbg_gt = dint("dbg_gt", [128, 512], BF16)
    dbg_y = dint("dbg_y", [128, 16384], F32)
    dbg_rt = dint("dbg_rt", [128, 160], F32)
    dbg_ye = dint("dbg_ye", [128, 8 * 512], F32)
    dbg_h2 = dint("dbg_h2", [128, 2048], BF16)
    dbg_y1 = dint("dbg_y1", [128, 16384], F32)

    c_ones_row = sb("c_ones_row", [1, 128], F32)
    c_ones_col = sb("c_ones_col", [128, 1], F32)
    c_idf = sb("c_idf", [128, 128], F32)
    c_idb = sb("c_idb", [128, 128], BF16)
    c_Dbase = sb("c_Dbase", [128, 512], F32)
    c_D64 = sb("c_D64", [64, 64], F32)
    c_eye4 = sb("c_eye4", [64, 256], F32)
    c_strL = sb("c_strL", [64, 64], F32)
    c_strU = sb("c_strU", [64, 64], F32)
    c_tail = sb("c_tail", [1, 64], F32)
    c_dec = sb("c_dec", [1, 4], F32)
    c_wins = sb("c_wins", [1, 2], F32)
    c_hf = sb("c_hf", [1, 2], I32)
    c_rsc = sb("c_rsc", [1, 1], F32)
    c_b2a = sb("c_b2a", [1, 1], F32)
    s_WgT = sb("s_WgT", [128, KC * NE], BF16)
    s_gWT1 = sb("s_gWT1", [128, KC * NE], BF16)
    s_b1 = sb("s_b1", [128, NE * 32], F32)
    s_b2 = sb("s_b2", [128, NE * KC], F32)
    s_b11 = sb("s_b11", [128, NE * 4], F32)
    s_b21 = sb("s_b21", [128, NE * KC], F32)
    s_bs1 = sb("s_bs1", [128, 4], F32)
    s_bs2 = sb("s_bs2", [128, KC], F32)
    s_bqkv = sb("s_bqkv", [128, 12], F32)
    s_W2a = sb("s_W2a", [128, 2], BF16)
    s_wv = sb("s_wv", [128, KC], BF16)
    s_thr = sb("s_thr", [128, 2 * 32], F32)
    s_hsc = sb("s_hsc", [128, 16], F32)
    s_gpb = sb("s_gpb", [128, 2 * 1024], BF16)
    s_gptail = sb("s_gptail", [64, 2], F32)
    s_Ls4 = sb("s_Ls4", [64, 2 * 256], F32)
    s_LsT4 = sb("s_LsT4", [64, 2 * 256], F32)
    s_srow = sb("s_srow", [1, 256], F32)
    s_bnd = sb("s_bnd", [128, 4], BF16)
    s_gre = sb("s_gre", [8, TH], BF16)
    s_rcol = sb("s_rcol", [128, 8], F32)
    s_sel8 = sb("s_sel8", [8, 8 * 128], BF16)

    ABW = sb("arena_w", [128, 12288], BF16)
    AFZ = sb("arena_f32", [128, 19456], F32)
    ABF = sb("arena_bf", [128, 40960], BF16)

    def af(off, n, p0=0, p1=128):
        return AFZ.ap()[p0:p1, off:off + n]

    def ab(off, n, p0=0, p1=128):
        return ABF.ap()[p0:p1, off:off + n]

    def afb(off_f32, n_bf, p0=0, p1=128):
        return AFZ.ap().bitcast(BF16)[p0:p1, 2 * off_f32: 2 * off_f32 + n_bf]

    def aw(off, n):
        return ABW.ap()[:, off:off + n]

    PS = [es.enter_context(nc.psum_tensor(f"ps{i}", [128, 512], F32)) for i in range(8)]
    PSB = [p.ap().bitcast(BF16) for p in PS]

    class Sem:
        def __init__(self, name):
            self.h = es.enter_context(nc.semaphore(name))
            self.n = 0

    LD = Sem("LD"); WL = Sem("WL"); WC = Sem("WC"); MM = Sem("MM"); EV = Sem("EV")
    SC = Sem("SC"); GP = Sem("GP"); CCs = Sem("CCs"); ST = Sem("ST")
    WCE = Sem("WCE"); STG = Sem("STG"); SPL = Sem("SPL"); GPX = Sem("GPX"); W0C = Sem("W0C")
    W1L = [Sem("W1L0"), Sem("W1L1")]; W2L = Sem("W2L")
    XNL = [Sem("XNL0"), Sem("XNL1")]
    LW1 = [Sem("LW10"), Sem("LW11")]; LW2 = [Sem("LW20"), Sem("LW21")]
    WQL = [Sem("WQL0"), Sem("WQL1")]; WKL = [Sem("WKL0"), Sem("WKL1")]

    prog = {k: [] for k in ("sync", "gpsimd", "tensor", "vector", "scalar")}
    SYNC, GPS, TEN, VEC, SCA = "sync", "gpsimd", "tensor", "vector", "scalar"

    _ENG_SEM = {VEC: EV, SCA: SC}

    def R(eng, fn, sem=None, k=1):
        if (RACE_DEBUG or SELF_WAIT) and eng in _ENG_SEM:
            es_ = _ENG_SEM[eng]
            if es_.n > 0:
                prog[eng].append(lambda E, h=es_.h, v=es_.n: E.wait_ge(h, v))
        if sem is None:
            prog[eng].append(fn)
        else:
            h = sem.h
            prog[eng].append(lambda E, fn=fn, h=h, k=k: fn(E).then_inc(h, k))
            sem.n += k

    def RW(eng, sem, val=None):
        v = sem.n if val is None else val
        if v > 0:
            prog[eng].append(lambda E, h=sem.h, v=v: E.wait_ge(h, v))

    RT = {}

    def mmR(lhsT, rhs, out, start, stop, sem=MM, k=1):
        R(TEN, lambda E, a=lhsT, b=rhs, o=out, s=start, p=stop:
          E.matmul(o, a, b, start=s, stop=p), sem, k)

    def cpy(eng, out, in_, sem=EV):
        R(eng, lambda E, o=out, i=in_: E.tensor_copy(o, i), sem)

    def tt(eng, out, a, b, op, sem=EV):
        R(eng, lambda E, o=out, x=a, y=b, p=op: E.tensor_tensor(o, x, y, p), sem)

    def ts(eng, out, in0, s1, op0, sem=EV):
        R(eng, lambda E, o=out, i=in0, a=s1, p0=op0:
          E.tensor_scalar(out=o, in0=i, scalar1=a, scalar2=None, op0=p0), sem)

    def ts2(eng, out, in0, s1, op0, s2, op1, sem=EV):
        R(eng, lambda E, o=out, i=in0, a=s1, b=s2, p0=op0, p1=op1:
          E.tensor_scalar(out=o, in0=i, scalar1=a, scalar2=b, op0=p0, op1=p1), sem)

    def stt(eng, out, in0, scalar, in1, op0, op1, sem=EV):
        R(eng, lambda E, o=out, i0=in0, s=scalar, i1=in1, p0=op0, p1=op1:
          E.scalar_tensor_tensor(out=o, in0=i0, scalar=s, in1=i1, op0=p0, op1=p1), sem)

    def sbounce(src, dst):
        RW(SCA, EV)
        R(SCA, lambda E, s_=src, d_=dst: E.activation(d_, s_, AF.Copy), SC)
        RW(VEC, SC)

    def act(out, in_, func, bias=0.0, scale=1.0, sem=SC):
        R(SCA, lambda E, o=out, i=in_, f=func, b=bias, s=scale:
          E.activation(o, i, f, bias=b, scale=s), sem)

    def red(out, in_, op, negate=False, sem=EV):
        R(VEC, lambda E, o=out, i=in_, p=op, ng=negate:
          E.tensor_reduce(out=o, in_=i, axis=AX, op=p, negate=ng), sem)

    for dst, src in [
        (c_ones_row, ones_row), (c_ones_col, ones_col), (c_idf, identf),
        (c_Dbase, Dbase_c), (c_D64, D64_c), (c_eye4, eye4_c), (c_strL, strL_c),
        (c_strU, strU_c), (c_tail, tail_c), (c_dec, dec_al), (s_sel8, sel8_c),
        (c_wins, wins), (c_hf, hfidx), (c_rsc, rsc), (c_b2a, b2a),
        (s_bs1, bs1c), (s_bs2, bs2c), (s_bqkv, bqkv_sl),
    ]:
        R(SYNC, lambda E, d=dst, s=src: E.dma_start(out=d[:], in_=s[:]), LD, 16)
    R(GPS, lambda E: E.dma_start(out=s_WgT[:], in_=WgT.rearrange("kc p e -> p kc e")), WC, 16)
    R(GPS, lambda E: E.dma_start(out=s_gWT1[:], in_=gWT1.rearrange("kc p e -> p kc e")), WC, 16)
    for e in range(NE):
        R(SYNC, lambda E, e=e: E.dma_start(out=s_b1[:, e * 32:(e + 1) * 32], in_=b1c[e]), LD, 16)
        R(SYNC, lambda E, e=e: E.dma_start(out=s_b2[:, e * KC:(e + 1) * KC], in_=b2c[e]), LD, 16)
        R(SYNC, lambda E, e=e: E.dma_start(out=s_b11[:, e * 4:(e + 1) * 4], in_=b1c1[e]), LD, 16)
        R(SYNC, lambda E, e=e: E.dma_start(out=s_b21[:, e * KC:(e + 1) * KC], in_=b2c1[e]), LD, 16)
    R(SYNC, lambda E: E.dma_start(out=af(11264, 1024, 0, 1), in_=prow_c[:]), LD, 16)
    P0_LD = LD.n
    c_prow = lambda: af(11264, 1024, 0, 1)
    s_grow = lambda: af(12288, 2048, 0, 1)

    R(GPS, lambda E: E.dma_start(out=c_idb[:], in_=identf[:]), WC, 16)
    R(GPS, lambda E: E.dma_start(out=aw(0, 4096), in_=WwT_sl.rearrange("kc p n -> p kc n")), WC, 16)
    R(GPS, lambda E: E.dma_start(out=aw(4096, 8192), in_=WrT_t.rearrange("kc p n -> p kc n")), WC, 16)
    R(GPS, lambda E: E.dma_start(out=s_W2a[:], in_=W2a[:]), WC, 16)
    P0_WC = WC.n
    RW(GPS, WC, P0_WC)
    R(GPS, lambda E: E.nop(), W0C, 1)
    GP_w0 = W0C.n
    WwT = lambda kc: aw(kc * 512, 512)
    WrT = lambda kc, dc: aw(4096 + kc * 1024 + dc * 128, 128)

    RW(SYNC, LD, P0_LD)

    def load_regs(E):
        _, regs = nc.values_load_multi_w_load_instructions(
            c_hf[0:1, :], engines=[mybir.EngineType.SP],
            min_val=0, max_val=1, skip_runtime_bounds_check=True)
        RT["hf"], RT["ohf"] = regs
    R(SYNC, load_regs)

    SCR = [lambda i=i: af(8192 + i * 512, 512) for i in range(6)]
    XB = lambda: af(0, 4096)
    XHB = lambda: af(4096, 2048)
    XNO = lambda: ab(0, 4096)
    XHO = lambda: ab(4096, 2048)

    def emit_ln_stats_block(xsrc, stat_dma_col=None):
        RW(TEN, EV)
        for kc in range(KC):
            mmR(c_ones_col[:, 0:1], xsrc(kc), PS[6][0:1, 0:512], kc == 0, kc == KC - 1)
        for kc in range(KC):
            RW(SCA, MM)
            act(SCR[kc % 2](), xsrc(kc), AF.Square)
            RW(TEN, SC)
            mmR(c_ones_col[:, 0:1], SCR[kc % 2](), PS[7][0:1, 0:512], kc == 0, kc == KC - 1)
        RW(VEC, MM)
        a_r = SCR[4]()[0:1, 0:512]
        b_r = SCR[5]()[0:1, 0:512]
        m_r = SCR[2]()[0:1, 0:512]
        ts(VEC, m_r, PS[6][0:1, 0:512], 1.0 / D, ALU.mult)
        ts(VEC, a_r, PS[7][0:1, 0:512], 1.0 / D, ALU.mult)
        tt(VEC, b_r, m_r, m_r, ALU.mult)
        tt(VEC, a_r, a_r, b_r, ALU.subtract)
        ts(VEC, a_r, a_r, 1e-5, ALU.add)
        RW(SCA, EV)
        act(a_r, a_r, AF.Sqrt)
        RW(VEC, SC)
        R(VEC, lambda E, a=a_r, hh=EV.h: E.reciprocal(a, a).then_inc(hh, 1))
        EV.n += 1
        tt(VEC, b_r, m_r, a_r, ALU.mult)
        ts(VEC, b_r, b_r, -1.0, ALU.mult)
        EVs = EV.n
        if stat_dma_col is not None:
            RW(SCA, EV, EVs)
            R(SCA, lambda E, cb=stat_dma_col, a=a_r: E.dma_start(
                out=stat1_hbm[0:1, cb:cb + 512], in_=a), ST, 16)
            R(SCA, lambda E, cb=stat_dma_col, b=b_r: E.dma_start(
                out=stat1_hbm[0:1, T + cb:T + cb + 512], in_=b), ST, 16)
            st_after["statst"] = ST.n
        RW(TEN, EV, EVs)
        mmR(c_ones_row[0:1, :], a_r, PS[4][:, 0:512], True, True)
        mmR(c_ones_row[0:1, :], b_r, PS[5][:, 0:512], True, True)

    x_nat = [x0_h0, x0_h1]
    st_after = {}
    for blk in range(8):
        half, t0 = blk // 4, (blk % 4) * 512
        RW(SYNC, EV, st_after.get("p1a_ev", 0))
        R(SYNC, lambda E, h=half, t0=t0: E.dma_start(
            out=XB(), in_=x_nat[h][:, :, t0:t0 + 512].rearrange("kc p t -> p kc t")), WL, 16)
        R(SYNC, lambda E, t0=blk * 512: E.dma_start(
            out=XHB(), in_=x0hg[:, :, t0:t0 + 512].rearrange("c p t -> p c t")), WL, 16)
        WLb = WL.n
        RW(TEN, WL, WLb)
        RW(SCA, WL, WLb)
        xsrc = lambda kc: XB()[:, kc * 512:(kc + 1) * 512]
        emit_ln_stats_block(xsrc)
        RW(VEC, MM)
        RW(VEC, ST)
        for kc in range(KC):
            tt(VEC, SCR[3](), xsrc(kc), PS[4][:, 0:512], ALU.mult)
            tt(VEC, XNO()[:, kc * 512:(kc + 1) * 512], SCR[3](), PS[5][:, 0:512], ALU.add)
        for ck in range(4):
            tt(VEC, SCR[3](), XHB()[:, ck * 512:(ck + 1) * 512], PS[4][:, 0:512], ALU.mult)
            tt(VEC, XHO()[:, ck * 512:(ck + 1) * 512], SCR[3](), PS[5][:, 0:512], ALU.add)
        st_after["p1a_ev"] = EV.n
        RW(SCA, EV)
        R(SCA, lambda E, t0=blk * 512: E.dma_start(
            out=xnn_hbm[:, :, t0:t0 + 512].rearrange("kc p t -> p kc t"), in_=XNO()), ST, 16)
        R(SCA, lambda E, t0=blk * 512: E.dma_start(
            out=xh_hbm[:, :, t0:t0 + 512].rearrange("c p t -> p c t"), in_=XHO()), ST, 16)
        st_after["p1a_st"] = ST.n
    ST_p1a = ST.n

    stgW = afb(0, 4096)
    RW(GPS, ST, ST_p1a)
    for e in range(NE):
        for g in range(16):
            RW(GPS, STG)
            R(GPS, lambda E, e=e, g=g: E.dma_start(
                out=stgW[:, 0:2048],
                in_=W1c[e, 2 * g:2 * g + 2].rearrange("ic p n -> p ic n")), WC, 16)
            RW(GPS, WC)
            R(GPS, lambda E, e=e, g=g: E.dma_start(
                out=w1cache[e, 2 * g:2 * g + 2].rearrange("ic p n -> p ic n"),
                in_=stgW[:, 0:2048]), STG, 16)
        for dc in range(KC):
            RW(GPS, STG)
            R(GPS, lambda E, e=e, dc=dc: E.dma_start(out=stgW[:, 0:4096], in_=W2c[e, dc]), WC, 16)
            RW(GPS, WC)
            R(GPS, lambda E, e=e, dc=dc: E.dma_start(out=w2cache[e, dc], in_=stgW[:, 0:4096]), STG, 16)
        RW(GPS, STG)
        R(GPS, lambda E: E.nop(), WCE, 1)

    RW(SCA, LD, P0_LD)
    act(s_srow[0:1, 0:2], c_dec[0:1, 0:2], AF.Sigmoid)
    RW(SCA, SC)
    act(s_srow[0:1, 2:4], s_srow[0:1, 0:2], AF.Ln)
    act(s_srow[0:1, 4:6], c_dec[0:1, 2:4], AF.Exp)
    SC_hc = SC.n
    RW(TEN, SC, SC_hc)
    for h in range(2):
        mmR(c_ones_row[0:1, :], s_srow[0:1, 2 + h:3 + h], PS[5][:, h:h + 1], True, True)
        mmR(c_ones_row[0:1, :], s_srow[0:1, 4 + h:5 + h], PS[5][:, 4 + h:5 + h], True, True)
    RW(VEC, MM)
    cpy(VEC, s_hsc[:, 0:2], PS[5][:, 0:2])
    cpy(VEC, s_hsc[:, 4:6], PS[5][:, 4:6])
    ts(VEC, s_hsc[:, 6:8], s_hsc[:, 0:2], -1.0, ALU.mult)
    RW(VEC, SC, SC_hc)
    for h in range(2):
        ts(VEC, s_srow[0:1, 140 + h:141 + h], s_srow[0:1, 2 + h:3 + h], 64.0, ALU.mult)
    EV_hc = EV.n
    RW(SCA, EV, EV_hc)
    for h in range(2):
        act(s_grow()[0:1, h * 1024:(h + 1) * 1024], c_prow()[0:1, :], AF.Exp,
            scale=s_srow[0:1, 2 + h:3 + h])
        act(s_srow[0:1, 8 + 64 * h:72 + 64 * h], c_tail[0:1, :], AF.Exp,
            scale=s_srow[0:1, 2 + h:3 + h])
        act(s_srow[0:1, 6 + h:7 + h], s_srow[0:1, 140 + h:141 + h], AF.Exp)
        act(af(8192 + 256 * h, 64, 0, 64), c_D64[:, :], AF.Exp, scale=s_hsc[0:64, h:h + 1])
        act(af(8192 + 256 * h + 64, 64, 0, 64), c_D64[:, :], AF.Exp, scale=s_hsc[0:64, 6 + h:7 + h])
    SC_hc2 = SC.n
    RW(TEN, SC, SC_hc2)
    for h in range(2):
        mmR(c_ones_row[0:1, :], s_grow()[0:1, h * 1024:h * 1024 + 512], PS[4][:, 0:512], True, True)
        mmR(c_ones_row[0:1, :], s_grow()[0:1, h * 1024 + 512:h * 1024 + 1024], PS[6][:, 0:512], True, True)
        mmR(s_srow[0:1, 8 + 64 * h:72 + 64 * h], c_ones_row[0:1, 0:1], PS[3][0:64, h:h + 1], True, True)
        mmR(c_ones_row[0:1, :], s_srow[0:1, 6 + h:7 + h], PS[3][:, 4 + h:5 + h], True, True)
        RW(VEC, MM)
        cpy(VEC, s_gpb[:, h * 1024:h * 1024 + 512], PS[4][:, 0:512])
        cpy(VEC, s_gpb[:, h * 1024 + 512:h * 1024 + 1024], PS[6][:, 0:512])
        RW(TEN, EV)
    mmR(c_ones_row[0:1, :], c_b2a[0:1, 0:1], PS[3][:, 6:7], True, True)
    RW(VEC, MM)
    cpy(VEC, s_gptail[0:64, 0:2], PS[3][0:64, 0:2])
    cpy(VEC, s_hsc[:, 2:4], PS[3][:, 4:6])
    cpy(VEC, s_hsc[:, 8:9], PS[3][:, 6:7])
    RW(VEC, SC, SC_hc2)
    for h in range(2):
        for r4 in range(4):
            tt(VEC, s_Ls4[0:64, h * 256 + r4 * 64:h * 256 + (r4 + 1) * 64],
               af(8192 + 256 * h, 64, 0, 64), c_strL[:, :], ALU.mult)
            tt(VEC, s_LsT4[0:64, h * 256 + r4 * 64:h * 256 + (r4 + 1) * 64],
               af(8192 + 256 * h + 64, 64, 0, 64), c_strU[:, :], ALU.mult)
    R(VEC, lambda E: E.memset(s_bnd[:, :], 0.0), EV)
    R(VEC, lambda E: E.memset(af(10240, 1024)[:, :], 0.0), EV)
    R(VEC, lambda E: E.memset(ab(32768, 1024)[:, :], 0.0), EV)
    EV_hc2 = EV.n
    RW(TEN, EV, EV_hc2)
    RW(SCA, EV, EV_hc2)
    RW(TEN, W0C, GP_w0)

    S_f = lambda h: af(10240 + h * 512, 512)
    ATq = lambda h: ab(33792 + h * 256, 256, 0, 64)
    INq = lambda h: ab(34816 + h * 256, 256, 0, 64)
    S_b = lambda h: ab(32768 + h * 512, 512)
    NW = lambda i: af(12288 + i * 256, 256, 0, 64)
    PTq = lambda h: af(13568 + h * 256, 256, 0, 64)
    Gq_f = af(14080, 256, 0, 64)
    Iq_f = af(14336, 256, 0, 64)
    XNs = lambda: ab(0, 8192)
    XHs = lambda: ab(8192, 4096)
    RKB = lambda h: ab(12288 + h * 2048, 2048)
    WKB = lambda h: ab(16384 + h * 2048, 2048)
    RGB = lambda h: ab(20480 + h * 2048, 2048)
    VTMq = ab(24576, 2048, 0, 64)
    UB = lambda h: ab(26624 + h * 256, 256, 0, 64)
    VNB = lambda h: ab(27136 + h * 256, 256, 0, 64)
    WGB = lambda h: ab(27648 + h * 256, 256, 0, 64)
    OTMB = ab(28160, 256, 0, 64)
    OFMS = ab(28672, 4096)

    mm_stripe_after = 0
    for st_i in range(4):
        t0 = st_i * 1024
        RW(SYNC, ST, ST_p1a)
        RW(SYNC, MM, mm_stripe_after)
        R(SYNC, lambda E, t0=t0: E.dma_start(
            out=XNs(), in_=xnn_hbm[:, :, t0:t0 + 1024].rearrange("kc p t -> p kc t")), WL, 16)
        R(SYNC, lambda E, t0=t0: E.dma_start(
            out=XHs(), in_=xh_hbm[:, :, t0:t0 + 1024].rearrange("c p t -> p c t")), WL, 16)
        WL_st = WL.n
        RW(TEN, WL, WL_st)
        RW(SCA, WL, WL_st)
        RW(VEC, WL, WL_st)
        for h in range(2):
            for sub in range(2):
                RW(SCA, MM)
                for kc in range(2):
                    act(SCR[kc](), XHs()[:, (2 * h + kc) * 1024 + sub * 512:
                                         (2 * h + kc) * 1024 + sub * 512 + 512], AF.Square)
                RW(TEN, SC)
                for kc in range(2):
                    mmR(c_ones_col[:, 0:1], SCR[kc](), PS[6][0:1, 0:512], kc == 0, kc == 1)
                n_r = SCR[2]()[0:1, 0:512]
                RW(SCA, MM)
                act(n_r, PS[6][0:1, 0:512], AF.Sqrt)
                RW(VEC, SC)
                ts(VEC, n_r, n_r, 1e-12, ALU.max)
                R(VEC, lambda E, a=n_r, hh=EV.h: E.reciprocal(a, a).then_inc(hh, 1))
                EV.n += 1
                RW(TEN, EV)
                mmR(c_ones_row[0:1, :], n_r, PS[4][:, 0:512], True, True)
                RW(VEC, MM)
                for kc in range(2):
                    xhs = XHs()[:, (2 * h + kc) * 1024 + sub * 512:(2 * h + kc) * 1024 + sub * 512 + 512]
                    rks = RKB(h)[:, kc * 1024 + sub * 512: kc * 1024 + sub * 512 + 512]
                    tt(VEC, rks, xhs, PS[4][:, 0:512], ALU.mult)
                    tt(VEC, RGB(h)[:, kc * 1024 + sub * 512: kc * 1024 + sub * 512 + 512],
                       rks, s_gpb[:, h * 1024 + sub * 512: h * 1024 + sub * 512 + 512], ALU.mult)
            for kc in range(2):
                cpy(VEC, WKB(h)[:, kc * 1024 + 1:kc * 1024 + 1024],
                    RKB(h)[:, kc * 1024:kc * 1024 + 1023])
                cpy(VEC, WKB(h)[:, kc * 1024:kc * 1024 + 1], s_bnd[:, h * 2 + kc:h * 2 + kc + 1])
                cpy(VEC, s_bnd[:, h * 2 + kc:h * 2 + kc + 1],
                    RKB(h)[:, kc * 1024 + 1023:kc * 1024 + 1024])
        EV_rk = EV.n
        RW(TEN, EV, EV_rk)

        for q in range(4):
            tc0 = q * 256
            ci0 = st_i * 16 + q * 4
            RW(TEN, EV, st_after.get("vtmq_ev", 0))
            RW(TEN, SC)
            for i4 in range(4):
                if i4 > 0:
                    RW(TEN, EV)
                for kc in range(KC):
                    mmR(XNs()[:, kc * 1024 + tc0 + i4 * 64: kc * 1024 + tc0 + (i4 + 1) * 64],
                        WwT(kc), PS[5][0:64, 0:512], kc == 0, kc == KC - 1)
                RW(VEC, MM)
                cpy(VEC, VTMq[0:64, i4 * 512:(i4 + 1) * 512], PS[5][0:64, 0:512])
            st_after["vtm"] = EV.n
            for h in range(2):
                sl = lambda B, kc, i4: B[:, kc * 1024 + tc0 + i4 * 64: kc * 1024 + tc0 + i4 * 64 + 64]
                for i4 in range(4):
                    for kc in range(2):
                        mmR(sl(WKB(h), kc, i4), sl(WKB(h), kc, i4),
                            PS[0][0:64, i4 * 64:(i4 + 1) * 64], kc == 0, kc == 1)
                    for kc in range(2):
                        mmR(sl(WKB(h), kc, i4), sl(RKB(h), kc, i4),
                            PS[1][0:64, i4 * 64:(i4 + 1) * 64], kc == 0, kc == 1)
                RW(VEC, MM)
                cpy(VEC, Gq_f, PS[0][0:64, 0:256])
                cpy(VEC, Iq_f, PS[1][0:64, 0:256])
                Ls4h = s_Ls4[0:64, h * 256:(h + 1) * 256]
                LsT4h = s_LsT4[0:64, h * 256:(h + 1) * 256]
                stt(VEC, NW(0), Gq_f, -1.0, Ls4h, ALU.mult, ALU.mult)
                stt(VEC, NW(1), Gq_f, -1.0, LsT4h, ALU.mult, ALU.mult)
                tt(VEC, INq(h), Iq_f, LsT4h, ALU.mult)
                tt(VEC, PTq(h), NW(1), c_eye4[0:64, :], ALU.add)
                RW(TEN, EV)
                cur, nxt = 0, 2
                for lvl in range(5):
                    for i4 in range(4):
                        mmR(NW(cur + 1)[0:64, i4 * 64:(i4 + 1) * 64],
                            NW(cur)[0:64, i4 * 64:(i4 + 1) * 64],
                            PS[2][0:64, i4 * 64:(i4 + 1) * 64], True, True)
                    if lvl < 4:
                        for i4 in range(4):
                            mmR(NW(cur)[0:64, i4 * 64:(i4 + 1) * 64],
                                NW(cur + 1)[0:64, i4 * 64:(i4 + 1) * 64],
                                PS[3][0:64, i4 * 64:(i4 + 1) * 64], True, True)
                    RW(VEC, MM)
                    cpy(VEC, NW(nxt), PS[2][0:64, 0:256])
                    if lvl < 4:
                        cpy(VEC, NW(nxt + 1), PS[3][0:64, 0:256])
                    RW(TEN, EV)
                    for i4 in range(4):
                        mmR(NW(nxt)[0:64, i4 * 64:(i4 + 1) * 64],
                            PTq(h)[0:64, i4 * 64:(i4 + 1) * 64],
                            PS[2][0:64, i4 * 64:(i4 + 1) * 64], True, True)
                    RW(VEC, MM)
                    tt(VEC, PTq(h), PTq(h), PS[2][0:64, 0:256], ALU.add)
                    RW(TEN, EV)
                    cur, nxt = nxt, cur
                cpy(VEC, ATq(h), PTq(h))
                RW(TEN, EV)
            RW(TEN, EV)
            for i4 in range(4):
                for h in range(2):
                    tcs = tc0 + i4 * 64
                    wsl = lambda B, kc: B[:, kc * 1024 + tcs: kc * 1024 + tcs + 64]
                    RW(TEN, EV)
                    for kc in range(2):
                        mmR(wsl(WKB(h), kc), S_b(h)[:, kc * 256:(kc + 1) * 256],
                            PS[2][0:64, 0:256], kc == 0, kc == 1)
                    RW(VEC, MM)
                    tt(VEC, UB(h), VTMq[0:64, i4 * 512 + h * 256: i4 * 512 + (h + 1) * 256],
                       PS[2][0:64, 0:256], ALU.subtract)
                    RW(TEN, EV)
                    mmR(ATq(h)[0:64, i4 * 64:(i4 + 1) * 64], UB(h), PS[3][0:64, 0:256], True, True)
                    RW(VEC, MM)
                    cpy(VEC, VNB(h), PS[3][0:64, 0:256])
                    EV_vn = EV.n
                    RW(TEN, EV, EV_vn)
                    for kc in range(2):
                        R(TEN, lambda E, h=h, kc=kc, tcs=tcs: E.transpose(
                            PSB[4][0:64, kc * 128:(kc + 1) * 128],
                            WKB(h)[:, kc * 1024 + tcs: kc * 1024 + tcs + 64], c_idb[:, :]), MM)
                    RW(VEC, MM)
                    ts(VEC, WGB(h), PSB[4][0:64, 0:256], s_gptail[0:64, h:h + 1], ALU.mult)
                    EV_wg = EV.n
                    for kc in range(2):
                        mmR(wsl(RGB(h), kc), S_b(h)[:, kc * 256:(kc + 1) * 256],
                            PS[5][0:64, 0:256], kc == 0, False)
                    mmR(INq(h)[0:64, i4 * 64:(i4 + 1) * 64], VNB(h), PS[5][0:64, 0:256],
                        False, True)
                    RW(SCA, MM)
                    act(OTMB, PS[5][0:64, 0:256], AF.Copy, scale=s_hsc[0:64, 4 + h:5 + h])
                    SC_o = SC.n
                    RW(TEN, EV, EV_wg)
                    mmR(WGB(h)[0:64, 0:128], VNB(h), PS[6][:, 0:256], True, True)
                    mmR(WGB(h)[0:64, 128:256], VNB(h), PS[6][:, 256:512], True, True)
                    RW(VEC, MM)
                    for kc in range(2):
                        stt(VEC, S_f(h)[:, kc * 256:(kc + 1) * 256],
                            S_f(h)[:, kc * 256:(kc + 1) * 256], s_hsc[:, 2 + h:3 + h],
                            PS[6][:, kc * 256:(kc + 1) * 256], ALU.mult, ALU.add)
                    cpy(VEC, S_b(h), S_f(h))
                    RW(TEN, SC, SC_o)
                    for kc in range(2):
                        R(TEN, lambda E, kc=kc: E.transpose(
                            PSB[7][:, kc * 64:(kc + 1) * 64], OTMB[0:64, kc * 128:(kc + 1) * 128],
                            c_idb[0:64, 0:64]), MM)
                    ci = ci0 + i4
                    RW(VEC, MM)
                    RW(VEC, ST)
                    for kc in range(2):
                        cpy(VEC, OFMS[:, (2 * h + kc) * 1024 + tcs: (2 * h + kc) * 1024 + tcs + 64],
                            PSB[7][:, kc * 64:(kc + 1) * 64])
            st_after["vtmq_ev"] = EV.n
        RW(SCA, EV)
        R(SCA, lambda E, t0=t0: E.dma_start(
            out=ag1a_in.rearrange("(c p) t -> p c t", p=128)[:, :, t0:t0 + 1024],
            in_=OFMS.rearrange("p (c t) -> p c t", c=4)), ST, 16)
        st_after["ofms_st"] = ST.n
        mm_stripe_after = MM.n
    ST_p1b = ST.n

    RW(GPS, ST, ST_p1b)
    R(GPS, lambda E: E.collective_compute(
        "AllGather", ALU.bypass, replica_groups=[[0, 1], [2, 3], [4, 5], [6, 7]],
        ins=[ag1a_in[:]], outs=[ag1a_out[:]]), CCs, 1)
    CC_1a = CCs.n

    xnn_v = xnn_hbm.rearrange("kc p (hf t) -> kc p hf t", hf=2)
    xn1n_v = xn1n_hbm.rearrange("kc p (hf t) -> kc p hf t", hf=2)
    HSB = ab(0, 16384)
    XNB2 = lambda par: ab(16384 + par * 4096, 4096)
    XNB4 = XNB2
    W1B = lambda par: ab(24576 + par * 2048, 2048)
    W2B = lambda par: ab(28672 + par * 2048, 2048)
    OSB = lambda par: ab(32768 + par * 512, 512)
    GTB = lambda par: ab(33792 + par * 512, 512)
    PAYB = lambda par: ab(34816 + par * 512, 512)
    YF = af(0, 16384)
    UPC = lambda i: af(16384 + i * 512, 512)

    def read_xn_own(tb, par, layer, key):
        src = xnn_v if layer == 0 else xn1n_v
        RW(SYNC, MM, st_after.get(key + str(par), 0))
        R(SYNC, lambda E, tb=tb, par=par, src=src: E.dma_start(
            out=XNB2(par),
            in_=src[:, :, RT["hf"], tb * 512:(tb + 1) * 512].rearrange("kc p t -> p kc t")), XNL[par], 16)

    def emit_router(gre, WgTs, tb, par, use_sig):
        for tk in range(4):
            RW(VEC, MM)
            for kc in range(KC):
                mmR(XNB2(par)[:, kc * 512 + tk * 128: kc * 512 + (tk + 1) * 128],
                    WgTs[:, kc * NE:(kc + 1) * NE],
                    PS[6][0:128, 0:NE], kc == 0, kc == KC - 1)
            l_t = SCR[0]()[:, 0:8]
            if use_sig:
                RW(SCA, MM)
                act(l_t, PS[6][0:128, 0:NE], AF.Sigmoid)
                RW(VEC, SC)
                ts(VEC, l_t, l_t, s_rcol[:, 7:8], ALU.mult)
            else:
                RW(VEC, MM)
                cpy(VEC, l_t, PS[6][0:128, 0:NE])
            red(s_rcol[:, 0:1], l_t, ALU.max, negate=True)
            sbounce(s_rcol[:, 0:1], s_rcol[:, 4:5])
            ts(VEC, SCR[1]()[:, 0:8], l_t, s_rcol[:, 4:5], ALU.add)
            ts2(VEC, SCR[2]()[:, 0:8], SCR[1]()[:, 0:8], 0.0, ALU.is_equal,
                -1e30, ALU.mult)
            tt(VEC, SCR[2]()[:, 0:8], SCR[1]()[:, 0:8], SCR[2]()[:, 0:8], ALU.add)
            red(s_rcol[:, 1:2], SCR[2]()[:, 0:8], ALU.max)
            sbounce(s_rcol[:, 1:2], s_rcol[:, 5:6])
            ts(VEC, SCR[3]()[:, 0:8], SCR[1]()[:, 0:8], s_rcol[:, 5:6], ALU.is_ge)
            if use_sig:
                ts(VEC, s_rcol[:, 2:3], s_rcol[:, 0:1], -2.0, ALU.mult)
                tt(VEC, s_rcol[:, 2:3], s_rcol[:, 1:2], s_rcol[:, 2:3], ALU.add)
                R(VEC, lambda E, hh=EV.h: E.reciprocal(s_rcol[:, 3:4], s_rcol[:, 2:3]).then_inc(hh, 1))
                EV.n += 1
                sbounce(s_rcol[:, 3:4], s_rcol[:, 6:7])
                tt(VEC, SCR[1]()[:, 0:8], l_t, SCR[3]()[:, 0:8], ALU.mult)
                ts(VEC, SCR[1]()[:, 0:8], SCR[1]()[:, 0:8], s_rcol[:, 6:7], ALU.mult)
            else:
                EV_pre = EV.n
                RW(SCA, EV, EV_pre)
                act(SCR[2]()[:, 0:8], SCR[1]()[:, 0:8], AF.Exp)
                act(s_rcol[:, 2:3], s_rcol[:, 1:2], AF.Exp)
                RW(VEC, SC)
                ts(VEC, s_rcol[:, 2:3], s_rcol[:, 2:3], 1.0, ALU.add)
                R(VEC, lambda E, hh=EV.h: E.reciprocal(s_rcol[:, 3:4], s_rcol[:, 2:3]).then_inc(hh, 1))
                EV.n += 1
                sbounce(s_rcol[:, 3:4], s_rcol[:, 6:7])
                tt(VEC, SCR[1]()[:, 0:8], SCR[2]()[:, 0:8], SCR[3]()[:, 0:8], ALU.mult)
                ts(VEC, SCR[1]()[:, 0:8], SCR[1]()[:, 0:8], s_rcol[:, 6:7], ALU.mult)
            EV_g = EV.n
            if not use_sig and tb == 0:
                RW(SCA, EV, EV_g)
                R(SCA, lambda E, tk=tk: E.dma_start(
                    out=dbg_rt[:, tk * 40:tk * 40 + 8], in_=SCR[1]()[:, 0:8]), ST, 16)
                R(SCA, lambda E, tk=tk: E.dma_start(
                    out=dbg_rt[:, tk * 40 + 8:tk * 40 + 16], in_=SCR[3]()[:, 0:8]), ST, 16)
                R(SCA, lambda E, tk=tk: E.dma_start(
                    out=dbg_rt[:, tk * 40 + 16:tk * 40 + 24], in_=s_rcol[:, 0:8]), ST, 16)
                R(SCA, lambda E, tk=tk: E.dma_start(
                    out=dbg_rt[:, tk * 40 + 24:tk * 40 + 32], in_=SCR[2]()[:, 0:8]), ST, 16)
            RW(TEN, EV, EV_g)
            R(TEN, lambda E: E.transpose(PS[5][0:8, 0:128], SCR[1]()[:, 0:8], c_idf[:, :]), MM)
            RW(VEC, MM)
            cpy(VEC, gre[0:8, tb * 512 + tk * 128: tb * 512 + (tk + 1) * 128],
                PS[5][0:8, 0:128])

    RW(SYNC, ST, ST_p1a)
    st_after["r0xn0"] = MM.n
    st_after["r0xn1"] = MM.n
    for tb in range(4):
        read_xn_own(tb, tb % 2, 0, "r0xn")
        RW(VEC, XNL[tb % 2])
        RW(TEN, XNL[tb % 2])
        emit_router(s_gre, s_WgT, tb, tb % 2, use_sig=False)
        st_after["r0xn" + str(tb % 2)] = MM.n
    st_after["moex0"] = MM.n
    st_after["moex1"] = MM.n
    RW(SCA, EV)
    R(SCA, lambda E: E.dma_start(out=dbg_gre[:], in_=s_gre[:, :]), ST, 16)

    def emit_moe(layer):
        nic = 32 if layer == 0 else 4
        hsb = HSB if layer == 0 else ab(0, 2048)
        gre = s_gre
        b1s, b2s = (s_b1, s_b2) if layer == 0 else (s_b11, s_b21)
        xkey = "moex" if layer == 0 else "moex1"
        for e in range(NE):
            if layer == 0:
                RW(SYNC, WCE, e + 1)
            for tb in range(4):
                par = tb % 2
                read_xn_own(tb, par, layer, xkey)
                GPx = XNL[par].n
                RW(TEN, EV)
                mmR(s_sel8[:, e * 128:(e + 1) * 128], gre[0:8, tb * 512:(tb + 1) * 512],
                    PS[4][:, 0:512], True, True)
                RW(VEC, MM)
                cpy(VEC, GTB(par), PS[4][:, 0:512])
                EV_g = EV.n
                if layer == 0 and e == 0 and tb == 0:
                    RW(SCA, EV, EV_g)
                    R(SCA, lambda E, par=par: E.dma_start(out=dbg_gt[:], in_=GTB(par)), ST, 16)
                RW(TEN, XNL[par], GPx)
                for ic in range(nic):
                    hpar = ic % 2
                    if layer == 0:
                        RW(SYNC, MM, st_after.get(f"w1{hpar}", 0))
                        R(SYNC, lambda E, e=e, ic=ic, hpar=hpar: E.dma_start(
                            out=W1B(hpar)[:, 0:1024], in_=w1cache[e, ic]), W1L[hpar], 16)
                        RW(TEN, W1L[hpar])
                        wt = W1B(hpar)
                    else:
                        RW(GPS, MM, st_after.get(f"w1{hpar}", 0))
                        R(GPS, lambda E, e=e, ic=ic, hpar=hpar: E.dma_start(
                            out=W1B(hpar)[:, 0:1024], in_=W1c1[e, ic]), LW1[hpar], 16)
                        RW(TEN, LW1[hpar])
                        wt = W1B(hpar)
                    RW(TEN, SC, st_after.get(f"h{hpar}", 0))
                    for kc in range(KC):
                        mmR(wt[:, kc * 128:(kc + 1) * 128],
                            XNB2(par)[:, kc * 512:(kc + 1) * 512],
                            PS[hpar][:, 0:512], kc == 0, kc == KC - 1)
                    st_after[f"w1{hpar}"] = MM.n
                    RW(SCA, MM)
                    act(hsb[:, ic * 512:(ic + 1) * 512], PS[hpar][:, 0:512], AF.Gelu,
                        bias=b1s[:, e * nic + ic: e * nic + ic + 1])
                    st_after[f"h{hpar}"] = SC.n
                    if layer == 0 and e == 3 and tb == 1 and ic < 4:
                        R(SCA, lambda E, ic=ic: E.dma_start(
                            out=dbg_h2[:, ic * 512:(ic + 1) * 512],
                            in_=hsb[:, ic * 512:(ic + 1) * 512]), ST, 16)
                    if layer == 0 and e == 0 and tb == 0 and ic < 4:
                        R(SCA, lambda E, ic=ic: E.dma_start(
                            out=dbg_h[:, ic * 512:(ic + 1) * 512],
                            in_=hsb[:, ic * 512:(ic + 1) * 512]), ST, 16)
                SC_h = SC.n
                for dc in range(KC):
                    opar = dc % 2
                    if layer == 0:
                        for halfw in range(2):
                            RW(SYNC, MM, st_after.get(f"w2{halfw}", 0))
                            R(SYNC, lambda E, e=e, dc=dc, halfw=halfw: E.dma_start(
                                out=W2B(halfw),
                                in_=w2cache[e, dc, :, halfw * 2048:(halfw + 1) * 2048]), W2L, 16)
                        RW(TEN, W2L)
                        RW(TEN, SC, SC_h)
                        RW(TEN, SC, st_after.get(f"ob{opar}", 0))
                        for ic in range(32):
                            mmR(W2B(ic // 16)[:, (ic % 16) * 128:(ic % 16 + 1) * 128],
                                hsb[:, ic * 512:(ic + 1) * 512],
                                PS[2 + opar][:, 0:512], ic == 0, ic == 31)
                        st_after["w20"] = MM.n
                        st_after["w21"] = MM.n
                    else:
                        RW(GPS, MM, st_after.get(f"w2l1{opar}", 0))
                        R(GPS, lambda E, e=e, dc=dc, opar=opar: E.dma_start(
                            out=W2B(opar)[:, 0:512], in_=W2c1[e, dc]), LW2[opar], 16)
                        RW(TEN, LW2[opar])
                        RW(TEN, SC, SC_h)
                        RW(TEN, SC, st_after.get(f"ob{opar}", 0))
                        for ic in range(4):
                            mmR(W2B(opar)[:, ic * 128:(ic + 1) * 128],
                                hsb[:, ic * 512:(ic + 1) * 512],
                                PS[2 + opar][:, 0:512], ic == 0, ic == 3)
                        st_after[f"w2l1{opar}"] = MM.n
                    RW(SCA, MM)
                    RW(SCA, EV, st_after.get(f"osb{opar}", 0))
                    act(OSB(opar), PS[2 + opar][:, 0:512], AF.Identity,
                        bias=b2s[:, e * KC + dc: e * KC + dc + 1])
                    st_after[f"ob{opar}"] = SC.n
                    if layer == 0 and e == 0 and tb == 0 and dc == 0:
                        R(SCA, lambda E, opar=opar: E.dma_start(out=dbg_o[:], in_=OSB(opar)), ST, 16)
                        R(SCA, lambda E, par=par: E.dma_start(out=dbg_xnb[:], in_=XNB2(par)), ST, 16)
                    RW(VEC, SC)
                    RW(VEC, EV, EV_g)
                    ysl = YF[:, dc * 2048 + tb * 512: dc * 2048 + (tb + 1) * 512]
                    if e == 0 and layer == 0:
                        tt(VEC, ysl, OSB(opar), GTB(par), ALU.mult)
                    else:
                        tt(VEC, UPC(2), OSB(opar), GTB(par), ALU.mult)
                        tt(VEC, ysl, ysl, UPC(2), ALU.add)
                    st_after[f"osb{opar}"] = EV.n
                    if layer == 0 and tb == 0 and dc == 0:
                        RW(SCA, EV)
                        R(SCA, lambda E, e=e: E.dma_start(
                            out=dbg_ye[:, e * 512:(e + 1) * 512],
                            in_=YF[:, 0:512]), ST, 16)
                st_after[f"gt{par}"] = EV.n
                st_after[xkey + str(par)] = MM.n

    emit_moe(0)
    RW(SCA, EV)
    R(SCA, lambda E: E.dma_start(out=dbg_y[:], in_=YF), ST, 16)
    for dc in range(KC):
        for tb in range(4):
            par = (dc * 4 + tb) % 2
            ysl = YF[:, dc * 2048 + tb * 512: dc * 2048 + (tb + 1) * 512]
            RW(SCA, EV)
            act(ysl, ysl, AF.Silu)
            RW(VEC, SC)
            RW(VEC, ST)
            cpy(VEC, PAYB(par), ysl)
            RW(SCA, EV)
            R(SCA, lambda E, dc=dc, tb=tb, ysl=ysl: E.dma_start(
                out=g0_hbm[dc, :, tb * 512:(tb + 1) * 512], in_=ysl), ST, 16)
            R(SCA, lambda E, dc=dc, tb=tb, par=par: E.dma_start(
                out=ag1b_in.rearrange("(kc p) t -> kc p t", p=128)[dc, :, tb * 512:(tb + 1) * 512],
                in_=PAYB(par)), ST, 16)
            st_after[f"pay{par}"] = ST.n
    ST_p2 = ST.n

    RW(GPS, ST, ST_p2)
    R(GPS, lambda E: E.collective_compute(
        "AllGather", ALU.bypass, replica_groups=[[0, 1], [2, 3], [4, 5], [6, 7]],
        ins=[ag1b_in[:]], outs=[ag1b_out[:]]), CCs, 1)
    CC_1b = CCs.n
    RW(GPS, CCs, CC_1b)
    RW(GPS, CCs, CC_1a)
    R(GPS, lambda E: E.nop(), SPL, 1)

    ag1a_v = ag1a_out.rearrange("(r c p) t -> r c p t", c=4, p=128)
    ag1b_v = ag1b_out.rearrange("(r kc p) t -> r kc p t", kc=KC, p=128)
    OBK = lambda: ab(0, 4096)
    G0N = lambda: ab(4096, 4096)
    X1S = lambda: ab(8192, 4096)
    S0F = lambda: af(0, 4096)
    XNF = lambda: af(4096, 4096)
    G1F = lambda: af(15360, 4096)

    for blk in range(8):
        half, t0 = blk // 4, (blk % 4) * 512
        RW(SYNC, SPL, 1)
        RW(SYNC, MM, st_after.get("p3a_mm", 0))
        RW(SYNC, ST)
        for r in range(2):
            R(SYNC, lambda E, r=r, blk=blk: E.dma_start(
                out=OBK()[:, r * 2048:(r + 1) * 2048],
                in_=ag1a_v[r, :, :, blk * 512:(blk + 1) * 512].rearrange("c p t -> p c t")), WL, 16)
        R(SYNC, lambda E, half=half, t0=t0: E.dma_start(
            out=G0N(), in_=ag1b_v[half, :, :, t0:t0 + 512].rearrange("kc p t -> p kc t")), WL, 16)
        R(SYNC, lambda E, blk=blk: E.dma_start(
            out=X1S(), in_=xnn_hbm[:, :, blk * 512:(blk + 1) * 512].rearrange("kc p t -> p kc t")), WL, 16)
        R(SYNC, lambda E, half=half, t0=t0: E.dma_start(
            out=XNF(), in_=x_nat[half][:, :, t0:t0 + 512].rearrange("kc p t -> p kc t")), WL, 16)
        WL_b = WL.n
        RW(TEN, WL, WL_b)
        RW(VEC, WL, WL_b)
        for dc in range(KC):
            ppar = dc % 2
            RW(TEN, EV, st_after.get(f"p3p{ppar}", 0))
            for okc in range(KC):
                mmR(WrT(okc, dc), OBK()[:, okc * 512:(okc + 1) * 512],
                    PS[ppar][:, 0:512], okc == 0, okc == KC - 1)
            RW(VEC, MM)
            tt(VEC, S0F()[:, dc * 512:(dc + 1) * 512],
               X1S()[:, dc * 512:(dc + 1) * 512], PS[ppar][:, 0:512], ALU.add)
            st_after[f"p3p{ppar}"] = EV.n
            tt(VEC, UPC(3), S0F()[:, dc * 512:(dc + 1) * 512],
               G0N()[:, dc * 512:(dc + 1) * 512], ALU.mult)
            tt(VEC, XNF()[:, dc * 512:(dc + 1) * 512],
               XNF()[:, dc * 512:(dc + 1) * 512], UPC(3), ALU.add)
        EV_x1 = EV.n
        RW(SCA, EV, EV_x1)
        R(SCA, lambda E, blk=blk: E.dma_start(
            out=s0_hbm[:, :, blk * 512:(blk + 1) * 512].rearrange("kc p t -> p kc t"),
            in_=S0F()), ST, 16)
        xsrc1 = lambda kc: XNF()[:, kc * 512:(kc + 1) * 512]
        RW(VEC, ST)
        emit_ln_stats_block(xsrc1, stat_dma_col=blk * 512)
        RW(VEC, MM)
        for kc in range(KC):
            cpy(VEC, X1S()[:, kc * 512:(kc + 1) * 512], xsrc1(kc))
        EV_x1s = EV.n
        RW(SCA, EV, EV_x1s)
        R(SCA, lambda E, blk=blk: E.dma_start(
            out=x1n_hbm[:, :, blk * 512:(blk + 1) * 512].rearrange("kc p t -> p kc t"),
            in_=X1S()), ST, 16)
        st_after["p3a_mm"] = MM.n
        st_after["p3a_st"] = ST.n
    ST_p3a = ST.n

    s0_v = s0_hbm.rearrange("kc p (hf t) -> kc p hf t", hf=2)
    for tb in range(4):
        RW(SYNC, ST, ST_p3a)
        RW(SYNC, EV, st_after.get("p3b_ev", 0))
        R(SYNC, lambda E, tb=tb: E.dma_start(
            out=S0F(), in_=s0_v[:, :, RT["hf"], tb * 512:(tb + 1) * 512].rearrange("kc p t -> p kc t")), GPX, 16)
        R(SYNC, lambda E, tb=tb: E.dma_start(
            out=XNF(), in_=x0_own[:, :, tb * 512:(tb + 1) * 512].rearrange("kc p t -> p kc t")), WL, 16)
        R(SYNC, lambda E, tb=tb: E.dma_start(
            out=G1F(), in_=g0_hbm[:, :, tb * 512:(tb + 1) * 512].rearrange("kc p t -> p kc t")), WL, 16)
        WL_o = WL.n
        RW(VEC, WL, WL_o)
        RW(VEC, GPX)
        RW(VEC, ST)
        for kc in range(KC):
            tt(VEC, SCR[0](), S0F()[:, kc * 512:(kc + 1) * 512],
               G1F()[:, kc * 512:(kc + 1) * 512], ALU.mult)
            tt(VEC, XNF()[:, kc * 512:(kc + 1) * 512],
               XNF()[:, kc * 512:(kc + 1) * 512], SCR[0](), ALU.add)
        st_after["p3b_ev"] = EV.n
        RW(SCA, EV)
        R(SCA, lambda E, tb=tb: E.dma_start(
            out=x1o_hbm[:, :, tb * 512:(tb + 1) * 512].rearrange("kc p t -> p kc t"),
            in_=XNF()), ST, 16)
        st_after["p3b_st"] = ST.n
    ST_p3b = ST.n

    for blk in range(8):
        RW(SYNC, ST, ST_p3b)
        RW(SYNC, EV, st_after.get("p4p_ev", 0))
        R(SYNC, lambda E, blk=blk: E.dma_start(
            out=X1S(), in_=x1n_hbm[:, :, blk * 512:(blk + 1) * 512].rearrange("kc p t -> p kc t")), WL, 16)
        R(SYNC, lambda E, blk=blk: E.dma_start(
            out=SCR[4]()[0:1, 0:512], in_=stat1_hbm[0:1, blk * 512:(blk + 1) * 512]), WL, 16)
        R(SYNC, lambda E, blk=blk: E.dma_start(
            out=SCR[5]()[0:1, 0:512], in_=stat1_hbm[0:1, T + blk * 512:T + (blk + 1) * 512]), WL, 16)
        WL_b = WL.n
        RW(TEN, WL, WL_b)
        RW(TEN, ST, ST_p3a)
        RW(TEN, EV, st_after.get("p4p_ev", 0))
        mmR(c_ones_row[0:1, :], SCR[4]()[0:1, 0:512], PS[4][:, 0:512], True, True)
        mmR(c_ones_row[0:1, :], SCR[5]()[0:1, 0:512], PS[5][:, 0:512], True, True)
        RW(VEC, MM)
        RW(VEC, WL, WL_b)
        RW(VEC, ST)
        for kc in range(KC):
            tt(VEC, UPC(3), X1S()[:, kc * 512:(kc + 1) * 512], PS[4][:, 0:512], ALU.mult)
            tt(VEC, OBK()[:, kc * 512:(kc + 1) * 512], UPC(3), PS[5][:, 0:512], ALU.add)
        st_after["p4p_ev"] = EV.n
        RW(SCA, EV)
        R(SCA, lambda E, blk=blk: E.dma_start(
            out=xn1n_hbm[:, :, blk * 512:(blk + 1) * 512].rearrange("kc p t -> p kc t"),
            in_=OBK()), ST, 16)
        st_after["p4p_st"] = ST.n
    ST_p4p = ST.n

    RW(TEN, LD, P0_LD)
    RW(TEN, EV)
    mmR(c_ones_row[0:1, :], c_rsc[0:1, 0:1], PS[5][:, 0:1], True, True)
    RW(VEC, MM)
    cpy(VEC, s_rcol[:, 7:8], PS[5][:, 0:1])
    RW(SYNC, ST, ST_p4p)
    for tb in range(4):
        read_xn_own(tb, tb % 2, 1, "r1xn")
        RW(VEC, XNL[tb % 2])
        RW(TEN, XNL[tb % 2])
        emit_router(s_gre, s_gWT1, tb, tb % 2, use_sig=True)
        st_after["r1xn" + str(tb % 2)] = MM.n
    st_after["shx0"] = MM.n
    st_after["shx1"] = MM.n
    RW(SCA, ST, ST_p2)
    HS1 = ab(0, 2048)
    for tb in range(4):
        par = tb % 2
        read_xn_own(tb, par, 1, "shx")
        RW(TEN, XNL[par])
        for ic in range(4):
            hpar = ic % 2
            RW(GPS, MM, st_after.get(f"sw1{hpar}", 0))
            R(GPS, lambda E, ic=ic, hpar=hpar: E.dma_start(out=W1B(hpar)[:, 0:1024], in_=Ws1c[ic]), LW1[hpar], 16)
            RW(TEN, LW1[hpar])
            RW(TEN, SC, st_after.get(f"h{hpar}", 0))
            for kc in range(KC):
                mmR(W1B(hpar)[:, kc * 128:(kc + 1) * 128],
                    XNB2(par)[:, kc * 512:(kc + 1) * 512], PS[hpar][:, 0:512],
                    kc == 0, kc == KC - 1)
            st_after[f"sw1{hpar}"] = MM.n
            RW(SCA, MM)
            act(HS1[:, ic * 512:(ic + 1) * 512], PS[hpar][:, 0:512], AF.Gelu,
                bias=s_bs1[:, ic:ic + 1])
            st_after[f"h{hpar}"] = SC.n
        SC_hs = SC.n
        for dc in range(KC):
            opar = dc % 2
            RW(GPS, MM, st_after.get(f"sw2{opar}", 0))
            R(GPS, lambda E, dc=dc, opar=opar: E.dma_start(out=W2B(opar)[:, 0:512], in_=Ws2c[dc]), LW2[opar], 16)
            RW(TEN, LW2[opar])
            RW(TEN, SC, SC_hs)
            RW(TEN, SC, st_after.get(f"ob{opar}", 0))
            for ic in range(4):
                mmR(W2B(opar)[:, ic * 128:(ic + 1) * 128],
                    HS1[:, ic * 512:(ic + 1) * 512], PS[2 + opar][:, 0:512],
                    ic == 0, ic == 3)
            st_after[f"sw2{opar}"] = MM.n
            RW(SCA, MM)
            act(YF[:, dc * 2048 + tb * 512: dc * 2048 + (tb + 1) * 512],
                PS[2 + opar][:, 0:512], AF.Identity, bias=s_bs2[:, dc:dc + 1])
            st_after[f"ob{opar}"] = SC.n
        st_after["shx" + str(par)] = MM.n
    st_after["moex10"] = MM.n
    st_after["moex11"] = MM.n
    RW(VEC, SC)
    RW(SCA, EV)
    R(SCA, lambda E: E.dma_start(out=dbg_y1[:], in_=YF), ST, 16)
    emit_moe(1)
    for dc in range(KC):
        for tb in range(4):
            ysl = YF[:, dc * 2048 + tb * 512: dc * 2048 + (tb + 1) * 512]
            RW(SCA, EV)
            act(ysl, ysl, AF.Silu)
            R(SCA, lambda E, dc=dc, tb=tb, ysl=ysl: E.dma_start(
                out=g1_hbm[dc, :, tb * 512:(tb + 1) * 512], in_=ysl), ST, 16)
    ST_p4 = ST.n

    sWop = ab(2048, 2048)
    RW(GPS, ST, ST_p4p)
    R(GPS, lambda E: E.dma_start(out=sWop.rearrange("p (l n) -> p l n", l=2),
                                 in_=Wop_t.rearrange("l p n -> p l n")), WC, 16)
    WC_wop = WC.n
    RW(TEN, WC, WC_wop)
    RW(TEN, EV)
    RW(TEN, SC)
    for dc in range(KC):
        for lc in range(2):
            mmR(sWop[:, lc * D + dc * 128: lc * D + (dc + 1) * 128],
                s_W2a[:, lc:lc + 1], PS[6][0:128, dc:dc + 1], lc == 0, lc == 1)
    RW(VEC, MM)
    cpy(VEC, s_wv[:, 0:KC], PS[6][0:128, 0:KC])
    EV_wv = EV.n
    RW(TEN, EV, EV_wv)
    WC_p4w = WC.n

    QIN = lambda: ab(4096, 4096)
    RQT = lambda: ab(8192, 512)
    QSTG = lambda: ab(8704, 3072)
    X1B = lambda: ab(0, 4096)
    WQB = lambda par: ab(11776, 2048)[:, par * 1024:(par + 1) * 1024]
    WKB5 = lambda par: ab(13824, 2048)[:, par * 1024:(par + 1) * 1024]
    QKVH = ab(16384, 24576)

    for blk in range(8):
        RW(SYNC, ST, ST_p4p)
        RW(SYNC, MM, st_after.get("p5_mm", 0))
        R(SYNC, lambda E, blk=blk: E.dma_start(
            out=X1B(), in_=xn1n_hbm[:, :, blk * 512:(blk + 1) * 512].rearrange("kc p t -> p kc t")), WL, 16)
        WL_b = WL.n
        RW(TEN, WL, WL_b)
        for dc in range(KC):
            qpar = dc % 2
            RW(GPS, MM, st_after.get(f"wq{qpar}", 0))
            R(GPS, lambda E, dc=dc, qpar=qpar: E.dma_start(
                out=WQB(qpar).rearrange("p (kc i) -> p kc i", kc=KC),
                in_=WqT_t[:, :, dc * 128:(dc + 1) * 128].rearrange("kc p i -> p kc i")), WQL[qpar], 16)
            RW(TEN, WQL[qpar])
            RW(TEN, SC, st_after.get(f"q{qpar}", 0))
            for kc in range(KC):
                mmR(WQB(qpar)[:, kc * 128:(kc + 1) * 128], X1B()[:, kc * 512:(kc + 1) * 512],
                    PS[qpar][:, 0:512], kc == 0, kc == KC - 1)
            st_after[f"wq{qpar}"] = MM.n
            RW(SCA, MM)
            act(QIN()[:, dc * 512:(dc + 1) * 512], PS[qpar][:, 0:512], AF.Copy)
            act(RQT(), PS[qpar][:, 0:512], AF.Relu)
            st_after[f"q{qpar}"] = SC.n
            RW(TEN, SC)
            for tk in range(4):
                mmR(RQT()[:, tk * 128:(tk + 1) * 128], s_wv[:, dc:dc + 1],
                    PS[[2, 3, 6, 7][tk]][0:128, 0:1], dc == 0, dc == KC - 1)
        SC_q = SC.n
        st_after["p5_mm"] = MM.n
        RW(SCA, MM)
        for tk in range(4):
            act(SCR[0]()[:, tk:tk + 1], PS[[2, 3, 6, 7][tk]][0:128, 0:1],
                AF.Sigmoid, bias=s_hsc[:, 8:9])
        RW(VEC, SC)
        for h in range(2):
            ts(VEC, s_thr[:, h * 32 + blk * 4: h * 32 + (blk + 1) * 4],
               SCR[0]()[:, 0:4], float(T), ALU.mult)
        RW(TEN, SC, SC_q)
        RW(SCA, ST)
        for ch in range(12):
            cpar = ch % 2
            RW(GPS, MM, st_after.get(f"wk5{cpar}", 0))
            R(GPS, lambda E, ch=ch, cpar=cpar: E.dma_start(
                out=WKB5(cpar).rearrange("p (kc i) -> p kc i", kc=KC),
                in_=Wqkv_sl[:, :, ch * 128:(ch + 1) * 128].rearrange("kc p i -> p kc i")), WKL[cpar], 16)
            RW(TEN, WKL[cpar])
            RW(TEN, SC, st_after.get(f"qk{cpar}", 0))
            for kc in range(KC):
                mmR(WKB5(cpar)[:, kc * 128:(kc + 1) * 128], QIN()[:, kc * 512:(kc + 1) * 512],
                    PS[4 + cpar][:, 0:512], kc == 0, kc == KC - 1)
            st_after[f"wk5{cpar}"] = MM.n
            RW(SCA, MM)
            if ch < 6:
                act(QKVH[:, ch * 4096 + blk * 512: ch * 4096 + (blk + 1) * 512],
                    PS[4 + cpar][:, 0:512], AF.Identity, bias=s_bqkv[:, ch:ch + 1])
            else:
                act(QSTG()[:, (ch - 6) * 512:(ch - 5) * 512], PS[4 + cpar][:, 0:512],
                    AF.Identity, bias=s_bqkv[:, ch:ch + 1])
            st_after[f"qk{cpar}"] = SC.n
        SC_qkv = SC.n
        R(SCA, lambda E, blk=blk: E.dma_start(
            out=qkv1_hbm[:, :, blk * 512:(blk + 1) * 512].rearrange("c p t -> p c t"),
            in_=QSTG().rearrange("p (c t) -> p c t", c=6)), ST, 16)
        st_after["p5_st"] = ST.n
    RW(TEN, EV)
    for h in range(2):
        mmR(c_ones_row[0:1, :], c_wins[0:1, h:h + 1], PS[6][:, h:h + 1], True, True)
    RW(VEC, MM)
    for h in range(2):
        ts(VEC, s_thr[:, h * 32:(h + 1) * 32], s_thr[:, h * 32:(h + 1) * 32],
           PS[6][:, h:h + 1], ALU.add)
    EV_thr = EV.n

    VTM1 = ab(0, 8192)
    PRW = ab(8192, 4096)
    PTF = ab(12288, 4096)
    ATS = lambda par: afb(4096, 512)[:, par * 256:(par + 1) * 256]

    RW(TEN, SC)
    RW(VEC, ST, ST_p4)
    for h in range(2):
        if h == 1:
            RW(SYNC, ST)
            RW(SYNC, MM, st_after.get("p6_mm", 0))
            for ch in range(6):
                R(SYNC, lambda E, ch=ch: E.dma_start(
                    out=QKVH[:, ch * 4096:(ch + 1) * 4096], in_=qkv1_hbm[ch]), WL, 16)
        WL_h = WL.n
        RW(TEN, WL, WL_h)
        RW(TEN, EV, EV_thr)
        qf = lambda kc: QKVH[:, kc * 4096: (kc + 1) * 4096]
        kf = lambda kc: QKVH[:, (2 + kc) * 4096: (3 + kc) * 4096]
        vf = lambda kc: QKVH[:, (4 + kc) * 4096: (5 + kc) * 4096]
        for ms in range(32):
            vpar = ms % 2
            RW(TEN, EV, st_after.get(f"vt{vpar}", 0))
            for ekc in range(2):
                R(TEN, lambda E, ms=ms, ekc=ekc, vpar=vpar: E.transpose(
                    PSB[2 + vpar][:, ekc * 128:(ekc + 1) * 128],
                    vf(ekc)[:, ms * 128:(ms + 1) * 128], c_idb[:, :]), MM)
            RW(VEC, MM)
            cpy(VEC, VTM1[:, ms * 256:(ms + 1) * 256], PSB[2 + vpar][:, 0:256])
            st_after[f"vt{vpar}"] = EV.n
        EV_vtm = EV.n
        RW(TEN, EV, EV_vtm)
        for lb in range(32):
            mc0 = lb // 4
            thr_col = s_thr[:, h * 32 + lb: h * 32 + lb + 1]
            RW(SCA, EV, st_after.get("p6red", 0))
            for mc in range(mc0, 8):
                spar = mc % 2
                RW(TEN, EV, st_after.get(f"sb{spar}", 0))
                for kc in range(2):
                    mmR(qf(kc)[:, lb * 128:(lb + 1) * 128], kf(kc)[:, mc * 512:(mc + 1) * 512],
                        PS[spar][:, 0:512], kc == 0, kc == 1)
                RW(VEC, MM)
                ts2(VEC, af(0, 4096)[:, mc * 512:(mc + 1) * 512], PS[spar][:, 0:512],
                    1.0 / 32.0, ALU.mult, BIGOFF, ALU.add)
                off = mc * 512 - lb * 128
                ts(VEC, s_rcol[:, 4:5], thr_col, float(off), ALU.subtract)
                sbounce(s_rcol[:, 4:5], s_rcol[:, 7:8])
                stt(VEC, af(0, 4096)[:, mc * 512:(mc + 1) * 512], c_Dbase[:, :], s_rcol[:, 7:8],
                    af(0, 4096)[:, mc * 512:(mc + 1) * 512], ALU.is_le, ALU.mult)
                if mc == mc0:
                    lo = float(lb * 128 - mc * 512)
                    stt(VEC, af(0, 4096)[:, mc * 512:(mc + 1) * 512], c_Dbase[:, :], lo,
                        af(0, 4096)[:, mc * 512:(mc + 1) * 512], ALU.is_ge, ALU.mult)
                st_after[f"sb{spar}"] = EV.n
            span0 = mc0 * 512
            red(s_rcol[:, 5:6], af(0, 4096)[:, span0:4096], ALU.max, negate=True)
            EV_mx = EV.n
            RW(SCA, EV, EV_mx)
            act(PRW[:, span0:4096], af(0, 4096)[:, span0:4096], AF.Exp, bias=s_rcol[:, 5:6])
            SC_exp = SC.n
            RW(VEC, SC, SC_exp)
            red(s_rcol[:, 6:7], PRW[:, span0:4096], ALU.add)
            R(VEC, lambda E, hh=EV.h: E.reciprocal(s_rcol[:, 6:7], s_rcol[:, 6:7]).then_inc(hh, 1))
            EV.n += 1
            sbounce(s_rcol[:, 6:7], s_rcol[:, 3:4])
            ts(VEC, PRW[:, span0:4096], PRW[:, span0:4096], s_rcol[:, 3:4], ALU.mult)
            st_after["p6red"] = EV.n
            EV_pn = EV.n
            RW(TEN, EV, EV_pn)
            for ms in range(mc0 * 4, 32):
                tpar = ms % 2
                RW(TEN, EV, st_after.get(f"pt{tpar}", 0))
                R(TEN, lambda E, ms=ms, tpar=tpar: E.transpose(
                    PSB[4 + tpar][:, 0:128], PRW[:, ms * 128:(ms + 1) * 128], c_idb[:, :]), MM)
                RW(VEC, MM)
                cpy(VEC, PTF[:, ms * 128:(ms + 1) * 128], PSB[4 + tpar][:, 0:128])
                st_after[f"pt{tpar}"] = EV.n
            EV_pt = EV.n
            RW(TEN, EV, EV_pt)
            first = True
            for ms in range(mc0 * 4, 32):
                for ekc in range(2):
                    mmR(VTM1[:, ms * 256 + ekc * 128: ms * 256 + (ekc + 1) * 128],
                        PTF[:, ms * 128:(ms + 1) * 128], PS[6 + ekc][:, 0:128],
                        first, ms == 31)
                first = False
            apar = lb % 2
            RW(VEC, MM)
            RW(VEC, ST)
            for ekc in range(2):
                cpy(VEC, ATS(apar)[:, ekc * 128:(ekc + 1) * 128], PS[6 + ekc][:, 0:128])
            RW(SCA, EV)
            R(SCA, lambda E, h=h, lb=lb, apar=apar: E.dma_start(
                out=ag2_in.rearrange("(c p) t -> c p t", p=128)[2 * h:2 * h + 2, :, lb * 128:(lb + 1) * 128].rearrange("c p t -> p c t"),
                in_=ATS(apar)[:, 0:256]), ST, 16)
            st_after[f"at{apar}"] = ST.n
        st_after["p6_st"] = ST.n
        st_after["p6_mm"] = MM.n
    ST_p6 = ST.n

    RW(GPS, ST, ST_p6)
    R(GPS, lambda E: E.collective_compute(
        "AllGather", ALU.bypass, replica_groups=[[0, 1], [2, 3], [4, 5], [6, 7]],
        ins=[ag2_in[:]], outs=[ag2_out[:]]), CCs, 1)
    CC_2 = CCs.n
    RW(GPS, CCs, CC_2)

    ag2_v = ag2_out.rearrange("(r c p) (hf t) -> r c p hf t", c=4, p=128, hf=2)
    S1B = lambda: ab(0, 4096)
    X1F = lambda: af(0, 4096)
    G1FF = lambda: af(4096, 4096)
    OUTF = lambda: af(8192, 4096)
    RW(SYNC, CCs, CC_2)
    for tb in range(4):
        RW(SYNC, EV, st_after.get("p7_ev", 0))
        for r in range(2):
            R(SYNC, lambda E, r=r, tb=tb: E.dma_start(
                out=S1B()[:, r * 2048:(r + 1) * 2048],
                in_=ag2_v[r, :, :, RT["hf"], tb * 512:(tb + 1) * 512].rearrange("c p t -> p c t")), GPX, 16)
        RW(SYNC, ST, ST_p3b)
        RW(SYNC, ST, ST_p4)
        R(SYNC, lambda E, tb=tb: E.dma_start(
            out=X1F(), in_=x1o_hbm[:, :, tb * 512:(tb + 1) * 512].rearrange("kc p t -> p kc t")), WL, 16)
        R(SYNC, lambda E, tb=tb: E.dma_start(
            out=G1FF(), in_=g1_hbm[:, :, tb * 512:(tb + 1) * 512].rearrange("kc p t -> p kc t")), WL, 16)
        WL_f = WL.n
        RW(VEC, WL, WL_f)
        RW(VEC, GPX)
        RW(VEC, ST)
        for kc in range(KC):
            cpy(VEC, UPC(4), S1B()[:, kc * 512:(kc + 1) * 512])
            tt(VEC, UPC(4), UPC(4), G1FF()[:, kc * 512:(kc + 1) * 512], ALU.mult)
            tt(VEC, OUTF()[:, kc * 512:(kc + 1) * 512],
               X1F()[:, kc * 512:(kc + 1) * 512], UPC(4), ALU.add)
        st_after["p7_ev"] = EV.n
        RW(SCA, EV)
        R(SCA, lambda E, tb=tb: E.dma_start(
            out=y_out[:, :, tb * 512:(tb + 1) * 512].rearrange("kc p t -> p kc t"),
            in_=OUTF()), ST, 16)
        st_after["p7_st"] = ST.n
    RW(SCA, ST)

    block = es.enter_context(nc.Block())
    for eng in (SYNC, GPS, TEN, VEC, SCA):
        def body(E, eng=eng):
            for fn in prog[eng]:
                fn(E)
        getattr(block, eng)(body)
    es.close()
    return nc


def _prep_inputs(ii):
    f = np.float32
    ca = np.ascontiguousarray
    x0 = ca(ii["L0_attn_out"], dtype=f)
    Ww = np.asarray(ii["L0_attn_Ww"], f); Wr = np.asarray(ii["L0_attn_Wr"], f)
    dec = np.asarray(ii["L0_attn_decay"], f); lal = np.asarray(ii["L0_attn_log_alpha"], f)
    Wg = np.asarray(ii["L0_ffn_Wg"], f)
    W10 = np.asarray(ii["L0_ffn_W1"], f); b10 = np.asarray(ii["L0_ffn_b1"], f)
    W20 = np.asarray(ii["L0_ffn_W2"], f); b20 = np.asarray(ii["L0_ffn_b2"], f)
    Wq = np.asarray(ii["L1_attn_Wq"], f)
    Wqkv = np.asarray(ii["L1_attn_Wqkv"], f); bqkv = np.asarray(ii["L1_attn_bqkv"], f)
    Wop = np.asarray(ii["L1_attn_Wop"], f); W2at = np.asarray(ii["L1_attn_W2"], f)
    b2at = np.asarray(ii["L1_attn_b2"], f)
    gW1 = np.asarray(ii["L1_ffn_gate_W"], f)
    W11 = np.asarray(ii["L1_ffn_W1"], f); b11 = np.asarray(ii["L1_ffn_b1"], f)
    W21 = np.asarray(ii["L1_ffn_W2"], f); b21 = np.asarray(ii["L1_ffn_b2"], f)
    Ws1 = np.asarray(ii["L1_ffn_Ws1"], f); bs1 = np.asarray(ii["L1_ffn_bs1"], f)
    Ws2 = np.asarray(ii["L1_ffn_Ws2"], f); bs2 = np.asarray(ii["L1_ffn_bs2"], f)
    rs = np.asarray(ii["L1_ffn_route_scale"], f)

    WgT = ca(Wg.T.reshape(KC, 128, NE))
    gWT1 = ca(gW1.reshape(KC, 128, NE))
    W1c = ca(W10.reshape(NE, KC, 128, 32, 128).transpose(0, 3, 2, 1, 4)
             .reshape(NE, 32, 128, KC * 128))
    W2c = ca(W20.reshape(NE, 32, 128, KC, 128).transpose(0, 3, 2, 1, 4)
             .reshape(NE, KC, 128, 32 * 128))
    b1c = ca(b10.reshape(NE, 32, 128).transpose(0, 2, 1))
    b2c = ca(b20.reshape(NE, KC, 128).transpose(0, 2, 1))
    WrT_t = ca(Wr.T.reshape(KC, 128, D))
    WqT_t = ca(Wq.T.reshape(KC, 128, D))
    Wop_t = ca(Wop.reshape(2, 128, D))
    W2a = ca(W2at.reshape(2, 128).T)
    W1c1 = ca(W11.reshape(NE, KC, 128, 4, 128).transpose(0, 3, 2, 1, 4)
              .reshape(NE, 4, 128, KC * 128))
    W2c1 = ca(W21.reshape(NE, 4, 128, KC, 128).transpose(0, 3, 2, 1, 4)
              .reshape(NE, KC, 128, 4 * 128))
    b1c1 = ca(b11.reshape(NE, 4, 128).transpose(0, 2, 1))
    b2c1 = ca(b21.reshape(NE, KC, 128).transpose(0, 2, 1))
    Ws1c = ca(Ws1.reshape(KC, 128, 4, 128).transpose(2, 1, 0, 3).reshape(4, 128, KC * 128))
    Ws2c = ca(Ws2.reshape(4, 128, KC, 128).transpose(2, 1, 0, 3).reshape(KC, 128, 4 * 128))
    bs1c = ca(bs1.reshape(4, 128).T)
    bs2c = ca(bs2.reshape(KC, 128).T)

    i32 = np.int32
    idx = np.arange(128, dtype=f)
    j512 = np.arange(512, dtype=f)
    shared = {
        "ones_row": np.ones((1, 128), f),
        "ones_col": np.ones((128, 1), f),
        "identf": np.eye(128, dtype=f),
        "Dbase_c": ca(j512[None, :] - idx[:, None]),
        "D64_c": ca(np.arange(64, dtype=f)[:, None] - np.arange(64, dtype=f)[None, :]),
        "eye4_c": ca(np.tile(np.eye(64, dtype=f), (1, 4))),
        "strL_c": ca(np.tril(np.ones((64, 64), f), -1)),
        "strU_c": ca(np.triu(np.ones((64, 64), f), 1)),
        "prow_c": ca(np.tile(np.arange(64, dtype=f), 16)[None, :]),
        "sel8_c": ca(np.repeat(np.eye(NE, dtype=f), 128, axis=1).astype(ml_dtypes.bfloat16)),
        "tail_c": ca((63.0 - np.arange(64, dtype=f))[None, :]),
        "WgT": WgT, "gWT1": gWT1, "W1c": W1c, "W2c": W2c, "b1c": b1c, "b2c": b2c,
        "WrT_t": WrT_t, "WqT_t": WqT_t, "Wop_t": Wop_t, "W2a": W2a,
        "b2a": b2at.reshape(1, 1),
        "W1c1": W1c1, "W2c1": W2c1, "b1c1": b1c1, "b2c1": b2c1,
        "Ws1c": Ws1c, "Ws2c": Ws2c, "bs1c": bs1c, "bs2c": bs2c,
        "rsc": rs.reshape(1, 1),
    }
    wins_all = np.array([32.0, 64.0, 128.0, 256.0], f)
    WqkvT = Wqkv.T

    in_maps = []
    for c in range(8):
        b, hf = c // 2, c % 2
        hg = hf
        hsl = slice(hg * 512, (hg + 1) * 512)
        xb = x0[b]
        m = dict(shared)
        m["x0_own"] = ca(xb[hf * TH:(hf + 1) * TH].T.reshape(KC, 128, TH))
        m["x0_h0"] = ca(xb[0:TH].T.reshape(KC, 128, TH))
        m["x0_h1"] = ca(xb[TH:].T.reshape(KC, 128, TH))
        m["x0hg"] = ca(xb[:, hg * 512:(hg + 1) * 512].T.reshape(4, 128, T))
        m["WwT_sl"] = ca(Ww.T[:, hsl].reshape(KC, 128, 2 * DH))
        m["dec_al"] = ca(np.concatenate([dec[2 * hg:2 * hg + 2],
                                         lal[2 * hg:2 * hg + 2]])[None, :])
        cols, bq = [], []
        for h in range(2):
            g = 2 * hg + h
            for qkv in range(3):
                r0 = qkv * D + g * DH
                cols.append(WqkvT[:, r0:r0 + DH])
                bq.append(bqkv[r0:r0 + DH].reshape(2, 128))
        m["Wqkv_sl"] = ca(np.concatenate(cols, axis=1).reshape(KC, 128, 12 * 128))
        m["bqkv_sl"] = ca(np.concatenate(bq, axis=0).T)
        m["wins"] = ca(wins_all[2 * hg:2 * hg + 2][None, :])
        m["hfidx"] = np.array([[hf, 1 - hf]], i32)
        in_maps.append(m)
    return in_maps


def kernel(**inputs):
    nc = build_nc()
    in_maps = _prep_inputs(inputs)
    res = run_bass_kernel_spmd(nc, in_maps, core_ids=list(range(8)))
    out = np.zeros((4, T, D), np.float32)
    for c in range(8):
        b, hf = c // 2, c % 2
        y = np.asarray(res.results[c]["y_out"])
        out[b, hf * TH:(hf + 1) * TH] = y.reshape(D, TH).T
    return out
